# version 04
# baseline (speedup 1.0000x reference)
import math
import numpy as np
import ml_dtypes
from contextlib import ExitStack

import concourse.bass as bass
import concourse.mybir as mybir
from concourse.bass_utils import run_bass_kernel_spmd

F32 = mybir.dt.float32
BF16 = mybir.dt.bfloat16
U8 = mybir.dt.uint8
AF = mybir.ActivationFunctionType
ALU = mybir.AluOpType
AX = mybir.AxisListType

NCORES = 8
T = 8192
D = 4096
DH = 128
FS = 512
HT = D // DH
IPAD = 11264
IS = IPAD // NCORES
NIT = IS // 128
JH = D // 2
JI = IS // 2
NJT_D = 6
SCALE = 1.0 / math.sqrt(32.0)
EPS = 1e-5
RG = [[0, 1, 2, 3, 4, 5, 6, 7]]

PCH = 512
NPCH = T // PCH
ACH = 512
NACH = T // ACH
FBLK = 2048
NFBLK = T // FBLK
DEBUG = False
ICHUNKS = [(0, 512), (512, 512), (1024, 384)]


def build_nc():
    nc = bass.Bass("TRN2", target_bir_lowering=False)

    d_x = nc.dram_tensor("xcol", [T, FS], F32, kind="ExternalInput").ap()
    d_wq = nc.dram_tensor("wq", [FS, D], F32, kind="ExternalInput").ap()
    d_bq = nc.dram_tensor("bq", [FS, 1], F32, kind="ExternalInput").ap()
    d_wk = nc.dram_tensor("wk", [FS, D], F32, kind="ExternalInput").ap()
    d_bk = nc.dram_tensor("bk", [FS, 1], F32, kind="ExternalInput").ap()
    d_wv = nc.dram_tensor("wv", [FS, D], F32, kind="ExternalInput").ap()
    d_bv = nc.dram_tensor("bv", [FS, 1], F32, kind="ExternalInput").ap()
    d_gpk = nc.dram_tensor("gpk", [IS, JH], U8, kind="ExternalInput").ap()
    d_upk = nc.dram_tensor("upk", [IS, JH], U8, kind="ExternalInput").ap()
    d_dpk = nc.dram_tensor("dpk", [D, JI], U8, kind="ExternalInput").ap()
    d_gs = nc.dram_tensor("gsT", [64, IS], F32, kind="ExternalInput").ap()
    d_gz = nc.dram_tensor("gzT", [64, IS], F32, kind="ExternalInput").ap()
    d_us = nc.dram_tensor("usT", [64, IS], F32, kind="ExternalInput").ap()
    d_uz = nc.dram_tensor("uzT", [64, IS], F32, kind="ExternalInput").ap()
    d_ds = nc.dram_tensor("dsT", [24, D], F32, kind="ExternalInput").ap()
    d_dz = nc.dram_tensor("dzT", [24, D], F32, kind="ExternalInput").ap()
    d_idb = nc.dram_tensor("identb", [128, 128], BF16, kind="ExternalInput").ap()
    d_idf = nc.dram_tensor("identf", [128, 128], F32, kind="ExternalInput").ap()
    d_o1b = nc.dram_tensor("ones1b", [1, 128], BF16, kind="ExternalInput").ap()
    d_o1f = nc.dram_tensor("ones1f", [1, 128], F32, kind="ExternalInput").ap()
    d_ocb = nc.dram_tensor("onescb", [128, 1], BF16, kind="ExternalInput").ap()
    d_masks = nc.dram_tensor("masks", [128, 4 * ACH], BF16, kind="ExternalInput").ap()

    d_out = nc.dram_tensor("outcol", [T, FS], F32, kind="ExternalOutput").ap()
    dbg = {}
    if DEBUG:
        for nm, shp, dt_ in [("xnt_fa", [D, T // 2], BF16),
                             ("xnt_b", [FS, T // 2], BF16),
                             ("xnt_b_early", [FS, T // 2], BF16),
                             ("fo_early", [FS, T], BF16),
                             ("fosh", [FS, T], BF16),
                             ("xnt_fb_early", [D, T // 2], BF16),
                             ("xnt_fb", [D, T // 2], BF16),
                             ("qt", [FS, T], BF16), ("kt", [FS, T], BF16),
                             ("vn", [T, FS], BF16), ("at", [FS, T], BF16),
                             ("r1", [FS, T], F32),
                             ("x2t_fa", [D, T // 2], BF16),
                             ("wg", [D, IS], BF16), ("wu", [D, IS], BF16),
                             ("wd", [IS, D], BF16), ("hs", [16 * IS, 512], BF16),
                             ("fp", [D, T], BF16), ("fo", [FS, T], BF16)]:
            dbg[nm] = nc.dram_tensor("dbg_" + nm, shp, dt_,
                                     kind="ExternalOutput").ap()

    i_ln1a = nc.dram_tensor("ln1a", [128, 128], F32, kind="Internal").ap()
    i_ln1b = nc.dram_tensor("ln1b", [128, 128], F32, kind="Internal").ap()
    i_ln2a = nc.dram_tensor("ln2a", [2, T], F32, kind="Internal").ap()
    i_ln2b = nc.dram_tensor("ln2b", [2, T], F32, kind="Internal").ap()
    i_xnt_a = nc.dram_tensor("xnt_a", [FS, T // 2], BF16, kind="Internal").ap()
    i_xnt_b = nc.dram_tensor("xnt_b", [FS, T // 2], BF16, kind="Internal").ap()
    i_xnt_fa = nc.dram_tensor("xnt_fa", [D, T // 2], BF16, kind="Internal").ap()
    i_xnt_fb = nc.dram_tensor("xnt_fb", [D, T // 2], BF16, kind="Internal").ap()
    i_xt = nc.dram_tensor("xt_scr", [FS, T], F32, kind="Internal").ap()
    i_qt = nc.dram_tensor("qt_scr", [FS, T], BF16, kind="Internal").ap()
    i_kt = nc.dram_tensor("kt_scr", [FS, T], BF16, kind="Internal").ap()
    i_vn = nc.dram_tensor("vn_scr", [T, FS], BF16, kind="Internal").ap()
    i_at = nc.dram_tensor("at_scr", [FS, T], BF16, kind="Internal").ap()
    i_r1 = nc.dram_tensor("r1_scr", [FS, T], F32, kind="Internal").ap()
    i_x2t_a = nc.dram_tensor("x2t_a", [FS, T // 2], BF16, kind="Internal").ap()
    i_x2t_b = nc.dram_tensor("x2t_b", [FS, T // 2], BF16, kind="Internal").ap()
    i_x2t_fa = nc.dram_tensor("x2t_fa", [D, T // 2], BF16, kind="Internal").ap()
    i_x2t_fb = nc.dram_tensor("x2t_fb", [D, T // 2], BF16, kind="Internal").ap()
    i_wg = nc.dram_tensor("wg_scr", [D, IS], BF16, kind="Internal").ap()
    i_wu = nc.dram_tensor("wu_scr", [D, IS], BF16, kind="Internal").ap()
    i_wd = nc.dram_tensor("wd_scr", [IS, D], BF16, kind="Internal").ap()
    i_hs = nc.dram_tensor("hs_scr", [16, IS, 512], BF16, kind="Internal").ap()
    i_fp = nc.dram_tensor("ffn_part", [D, T], BF16, kind="Internal").ap()
    i_fo = nc.dram_tensor("ffn_out", [FS, T], BF16, kind="Internal").ap()
    i_bar1 = nc.dram_tensor("bar1", [2, 16], F32, kind="Internal").ap()
    i_bar2 = nc.dram_tensor("bar2", [2, 16], F32, kind="Internal").ap()
    i_foshadow = nc.dram_tensor("foshadow", [FS, T], BF16, kind="Internal").ap()

    ENGS = ("sync", "gpsimd", "tensor", "vector", "scalar")
    Q = {e: [] for e in ENGS}
    CNT = {}
    SEMH = {}

    def emit(eng, fn):
        Q[eng].append(fn)

    def wait(eng, sem, val):
        if val and val > 0:
            emit(eng, lambda e, s=SEMH[sem], v=val: e.wait_ge(s, v))

    def bump(sem, amt=1):
        CNT[sem] = CNT.get(sem, 0) + amt
        return CNT[sem]

    def cur(sem):
        return CNT.get(sem, 0)

    def dma(eng, out, in_, sem):
        v = bump(sem, 16)
        emit(eng, lambda e, o=out, i=in_, s=SEMH[sem]:
             e.dma_start(out=o, in_=i).then_inc(s, 16))
        return v

    def op(eng, make, sem=None):
        if sem is None:
            emit(eng, make)
            return None
        v = bump(sem, 1)
        emit(eng, lambda e, m=make, s=SEMH[sem]: m(e).then_inc(s, 1))
        return v

    def cc(kind, aop, in_, out_):
        v = bump("cc", 1)
        emit("gpsimd", lambda e, k=kind, o=aop, i=in_, u=out_, s=SEMH["cc"]:
             e.collective_compute(k, o, replica_groups=RG, ins=[i.opt()],
                                  outs=[u.opt()]).then_inc(s, 1))
        return v

    ctx = ExitStack()
    with ctx:
        for s in ("sq", "gq", "cc", "pe", "dv", "ac"):
            SEMH[s] = ctx.enter_context(nc.semaphore(s))

        c_idb = ctx.enter_context(nc.sbuf_tensor("c_idb", [128, 128], BF16))
        c_idf = ctx.enter_context(nc.sbuf_tensor("c_idf", [128, 128], F32))
        c_o1b = ctx.enter_context(nc.sbuf_tensor("c_o1b", [1, 128], BF16))
        c_o1f = ctx.enter_context(nc.sbuf_tensor("c_o1f", [1, 128], F32))
        c_ocb = ctx.enter_context(nc.sbuf_tensor("c_ocb", [128, 1], BF16))
        c_eps = ctx.enter_context(nc.sbuf_tensor("c_eps", [128, 1], F32))
        c_scr = ctx.enter_context(nc.sbuf_tensor("c_scr", [1, 8], F32))

        ps_a = ctx.enter_context(nc.psum_tensor("ps_a", [128, 1024], F32))
        ps_b = ctx.enter_context(nc.psum_tensor("ps_b", [128, 1024], F32))
        ps_o = ctx.enter_context(nc.psum_tensor("ps_o", [128, 512], F32))
        ps_m = ctx.enter_context(nc.psum_tensor("ps_m", [128, 512], F32))
        ps_s = ctx.enter_context(nc.psum_tensor("ps_s", [1, 512], F32))
        ps_t = ctx.enter_context(nc.psum_tensor("ps_t", [128, 512], BF16))

        dma("sync", c_idb[:], d_idb, "sq")
        dma("sync", c_idf[:], d_idf, "sq")
        dma("sync", c_o1b[:], d_o1b, "sq")
        dma("sync", c_o1f[:], d_o1f, "sq")
        dma("sync", c_ocb[:], d_ocb, "sq")
        v_const = cur("sq")
        wait("tensor", "sq", v_const)
        op("vector", lambda e: e.memset(c_eps[:], EPS), "dv")
        v_eps = cur("dv")
        wait("scalar", "dv", v_eps)

        p0 = ExitStack()
        sb_x = p0.enter_context(nc.sbuf_tensor("sb_x", [128, 64 * 512], F32))
        sb_sqt = [p0.enter_context(nc.sbuf_tensor(f"sb_sqt{i}", [128, 512], F32))
                  for i in range(2)]
        sb_part = p0.enter_context(nc.sbuf_tensor("sb_part", [128, 128], F32))
        sb_stats = p0.enter_context(nc.sbuf_tensor("sb_stats", [128, 128], F32))
        sb_m = p0.enter_context(nc.sbuf_tensor("sb_m", [128, 64], F32))
        sb_e2 = p0.enter_context(nc.sbuf_tensor("sb_e2", [128, 64], F32))
        sb_msq = p0.enter_context(nc.sbuf_tensor("sb_msq", [128, 64], F32))
        sb_var = p0.enter_context(nc.sbuf_tensor("sb_var", [128, 64], F32))
        sb_sd = p0.enter_context(nc.sbuf_tensor("sb_sd", [128, 64], F32))
        sb_rs = p0.enter_context(nc.sbuf_tensor("sb_rs", [128, 64], F32))
        sb_xn4 = p0.enter_context(nc.sbuf_tensor("sb_xn4", [128, 4 * 512], BF16))
        sb_sgb = [p0.enter_context(nc.sbuf_tensor(f"sb_sgb{i}", [128, 512], BF16))
                  for i in range(2)]
        sb_sgf = [p0.enter_context(nc.sbuf_tensor(f"sb_sgf{i}", [128, 512], F32))
                  for i in range(2)]

        dma("sync", sb_x[:].rearrange("p (n m) -> p n m", m=512),
            d_x.rearrange("(n p) m -> p n m", p=128), "sq")
        v_xld = cur("sq")
        wait("scalar", "sq", v_xld)
        wait("vector", "sq", v_xld)
        for t in range(64):
            if t >= 2:
                wait("scalar", "dv", 2 * t - 2)
            op("scalar", lambda e, t=t: e.activation(
                sb_sqt[t % 2][:], sb_x[:, t * 512:(t + 1) * 512], AF.Square), "ac")
            op("vector", lambda e, t=t: e.reduce_sum(
                sb_part[:, 2 * t:2 * t + 1], sb_x[:, t * 512:(t + 1) * 512],
                axis=AX.X), "dv")
            wait("vector", "ac", t + 1)
            op("vector", lambda e, t=t: e.reduce_sum(
                sb_part[:, 2 * t + 1:2 * t + 2], sb_sqt[t % 2][:], axis=AX.X), "dv")
        wait("sync", "dv", 128)
        dma("sync", i_ln1a, sb_part[:], "sq")
        wait("gpsimd", "sq", cur("sq"))
        v_ar1 = cc("AllReduce", ALU.add, i_ln1a, i_ln1b)
        wait("sync", "cc", v_ar1)
        dma("sync", sb_stats[:], i_ln1b, "sq")
        v_stld = cur("sq")

        wait("vector", "sq", v_stld)
        op("vector", lambda e: e.tensor_single_scalar(
            sb_m[:], sb_stats[:, 0:128:2], 1.0 / D, ALU.mult), "dv")
        op("vector", lambda e: e.tensor_single_scalar(
            sb_e2[:], sb_stats[:, 1:128:2], 1.0 / D, ALU.mult), "dv")
        wait("scalar", "dv", cur("dv"))
        op("scalar", lambda e: e.activation(sb_msq[:], sb_m[:], AF.Square), "ac")
        wait("vector", "ac", cur("ac"))
        op("vector", lambda e: e.tensor_sub(sb_var[:], sb_e2[:], sb_msq[:]), "dv")
        wait("scalar", "dv", cur("dv"))
        op("scalar", lambda e: e.activation(sb_sd[:], sb_var[:], AF.Sqrt,
                                            bias=c_eps[:]), "ac")
        wait("vector", "ac", cur("ac"))
        op("vector", lambda e: e.reciprocal(sb_rs[:], sb_sd[:]), "dv")
        op("vector", lambda e: e.engine_nop(), None)
        op("vector", lambda e: e.engine_nop(), "dv")
        v_stats = cur("dv")

        v_dma0 = cur("sq")
        evac_b = {}
        evac_f = {}
        dma_done = {}
        xpose_pe = {}
        step = 0
        for g in range(16):
            if g == 0:
                wait("vector", "dv", v_stats)
            else:
                wait("vector", "pe", xpose_pe[g - 1])
            for j in range(4):
                t = 4 * g + j
                op("vector", lambda e, t=t, j=j: e.tensor_scalar(
                    sb_xn4[:, j * 512:(j + 1) * 512],
                    sb_x[:, t * 512:(t + 1) * 512],
                    sb_m[:, t:t + 1], sb_rs[:, t:t + 1],
                    ALU.subtract, ALU.mult), None)
            op("vector", lambda e: e.engine_nop(), "dv")
            v_norm = cur("dv")
            for ht in range(4):
                par = step % 2
                wait("tensor", "dv", v_norm)
                if step >= 1:
                    wait("tensor", "ac", evac_b[step - 1])
                for j in range(4):
                    op("tensor", lambda e, j=j, ht=ht: e.transpose(
                        ps_t[:, j * 128:(j + 1) * 128],
                        sb_xn4[:, j * 512 + ht * 128: j * 512 + ht * 128 + 128],
                        c_idb[:]), "pe" if j == 3 else None)
                v_tp1 = cur("pe")
                wait("scalar", "pe", v_tp1)
                if step >= 2:
                    wait("scalar", "sq", dma_done[step - 2])
                op("scalar", lambda e, par=par: e.copy(sb_sgb[par][:], ps_t[:]),
                   None)
                evac_b[step] = op("scalar", lambda e: e.copy(c_scr[0:1, 0:1], c_eps[0:1, 0:1]), "ac")
                if step >= 1:
                    wait("tensor", "ac", evac_f[step - 1])
                for j in range(4):
                    t = 4 * g + j
                    op("tensor", lambda e, j=j, ht=ht, t=t: e.transpose(
                        ps_m[:, j * 128:(j + 1) * 128],
                        sb_x[:, t * 512 + ht * 128: t * 512 + ht * 128 + 128],
                        c_idf[:]), "pe" if j == 3 else None)
                xpose_pe[g] = cur("pe")
                wait("scalar", "pe", xpose_pe[g])
                op("scalar", lambda e, par=par: e.copy(sb_sgf[par][:], ps_m[:]),
                   None)
                evac_f[step] = op("scalar", lambda e: e.copy(c_scr[0:1, 1:2], c_eps[0:1, 0:1]), "ac")
                wait("sync", "ac", evac_f[step])
                half = i_xnt_a if g < 8 else i_xnt_b
                coff = (g % 8) * 512
                dma("sync", half[ht * 128:(ht + 1) * 128, coff:coff + 512],
                    sb_sgb[par][:], "sq")
                dma("sync", i_xt[ht * 128:(ht + 1) * 128, g * 512:(g + 1) * 512],
                    sb_sgf[par][:], "sq")
                dma_done[step] = cur("sq")
                step += 1
        wait("gpsimd", "sq", cur("sq"))
        wait("gpsimd", "cc", v_ar1)
        v_ag1a = cc("AllGather", ALU.bypass, i_xnt_a, i_xnt_fa)
        wait("gpsimd", "cc", v_ag1a)
        wait("gpsimd", "sq", cur("sq"))
        v_ag1b = cc("AllGather", ALU.bypass, i_xnt_b, i_xnt_fb)
        v_p0 = dict(pe=cur("pe"), dv=cur("dv"), ac=cur("ac"), sq=cur("sq"),
                    gq=cur("gq"))
        p0.close()

        p1 = ExitStack()
        sb_w1 = p1.enter_context(nc.sbuf_tensor("sb_w1", [128, HT * 1024], BF16))
        sb_xnt = [p1.enter_context(nc.sbuf_tensor(f"sb_xnt{i}", [128, HT * PCH],
                                                  BF16)) for i in range(2)]
        sb_wld = p1.enter_context(nc.sbuf_tensor("sb_wld", [128, 4 * D], BF16))
        sb_pstg = [p1.enter_context(nc.sbuf_tensor(f"sb_pstg{i}", [128, PCH],
                                                   BF16)) for i in range(4)]
        sb_bias = p1.enter_context(nc.sbuf_tensor("sb_bias", [128, 12], F32))
        sb_vtt = p1.enter_context(nc.sbuf_tensor("sb_vtt", [128, 512], BF16))

        for s in ("pe", "dv", "ac"):
            wait("sync", s, v_p0[s])
        for bi, d_b in ((0, d_bq), (4, d_bk), (8, d_bv)):
            for t in range(4):
                dma("sync", sb_bias[:, bi + t:bi + t + 1],
                    d_b[t * 128:(t + 1) * 128, :], "sq")
        v_bias = cur("sq")
        wait("scalar", "sq", v_bias)

        def weight_prep(d_w, wslot, gate_pe=0):
            wait("gpsimd", "pe", max(gate_pe, cur("pe")))
            wait("gpsimd", "dv", cur("dv"))
            wait("gpsimd", "ac", cur("ac"))
            for dt in range(4):
                dma("gpsimd", sb_wld[:, dt * D:(dt + 1) * D],
                    d_w[dt * 128:(dt + 1) * 128, :], "gq")
            v_w = cur("gq")
            wait("tensor", "gq", v_w)
            for ht in range(HT):
                wait("tensor", "ac", cur("ac"))
                for dt in range(4):
                    op("tensor", lambda e, dt=dt, ht=ht: e.transpose(
                        ps_t[:, dt * 128:(dt + 1) * 128],
                        sb_wld[:, dt * D + ht * 128: dt * D + ht * 128 + 128],
                        c_idb[:]), "pe" if dt == 3 else None)
                wait("scalar", "pe", cur("pe"))
                op("scalar", lambda e, ht=ht, w=wslot: e.copy(
                    sb_w1[:, ht * 1024 + w * 512: ht * 1024 + w * 512 + 512],
                    ps_t[:]), "ac")

        if DEBUG:
            wait("sync", "cc", v_ag1b)
            dma("sync", dbg["xnt_b_early"], i_xnt_b, "sq")
            dma("sync", dbg["xnt_fb_early"], i_xnt_fb, "sq")
        weight_prep(d_wq, 0, gate_pe=v_p0["pe"])
        weight_prep(d_wk, 1)
        v_w1ready = cur("ac")

        PS4 = [ps_a[:, 0:512], ps_a[:, 512:1024], ps_b[:, 0:512], ps_b[:, 512:1024]]
        chunk_pe = {}
        stg_last = {}
        v_round_evac = v_w1ready
        pair_evac = {0: 0, 1: 0}

        for ci in range(NPCH):
            buf = ci % 2
            src = i_xnt_fa if ci < 8 else i_xnt_fb
            wait("sync", "cc", v_ag1a if ci < 8 else v_ag1b)
            if ci >= 2:
                wait("sync", "pe", chunk_pe[(0, ci - 2)])
            coff = (ci % 8) * PCH
            dma("sync", sb_xnt[buf][:].rearrange("p (n m) -> p n m", m=PCH),
                src[:, coff:coff + PCH].rearrange("(n p) m -> p n m", p=128), "sq")
            v_ld = cur("sq")
            wait("tensor", "sq", v_ld)
            for rnd in range(2):
                if ci == 0:
                    wait("tensor", "ac", v_w1ready)
                if pair_evac[rnd]:
                    wait("tensor", "dv", pair_evac[rnd])
                psr = [ps_a[:, 0:256], ps_a[:, 256:512], ps_a[:, 512:768],
                       ps_a[:, 768:1024]] if False else None
                PSR = ([ps_a[:, 0:512], ps_a[:, 512:1024],
                        ps_b[:, 0:512], ps_b[:, 512:1024]])
                base = 0 if rnd == 0 else 2
                for dt in range(4):
                    ptile = PSR[base + dt % 2]
                    if dt == 2:
                        wait("tensor", "dv", cur("dv"))
                    for ht in range(HT):
                        op("tensor", lambda e, dt=dt, ht=ht, rnd=rnd, buf=buf,
                           ptile=ptile: e.matmul(
                            ptile,
                            sb_w1[:, ht * 1024 + rnd * 512 + dt * 128:
                                  ht * 1024 + rnd * 512 + dt * 128 + 128],
                            sb_xnt[buf][:, ht * PCH:(ht + 1) * PCH],
                            start=(ht == 0), stop=(ht == HT - 1)),
                            "pe" if ht == HT - 1 else None)
                    if dt == 1 or dt == 3:
                        v_mm = cur("pe")
                        wait("vector", "pe", v_mm)
                        for dd in (dt - 1, dt):
                            prev = stg_last.get(dd)
                            if prev:
                                wait("vector", "sq", prev)
                            bcol = dd if rnd == 0 else 4 + dd
                            op("vector", lambda e, dd=dd, bcol=bcol,
                               pt=PSR[base + dd % 2]: e.tensor_scalar_add(
                                sb_pstg[dd][:], pt,
                                sb_bias[:, bcol:bcol + 1]), "dv")
                            wait("sync", "dv", cur("dv"))
                            dst = i_qt if rnd == 0 else i_kt
                            dma("sync", dst[dd * 128:(dd + 1) * 128,
                                            ci * PCH:(ci + 1) * PCH],
                                sb_pstg[dd][:], "sq")
                            stg_last[dd] = cur("sq")
                pair_evac[rnd] = cur("dv")
            chunk_pe[(0, ci)] = cur("pe")

        weight_prep(d_wv, 0, gate_pe=chunk_pe[(0, NPCH - 1)])
        v_wvready = cur("ac")
        first_v = True
        vt_dv = {}
        for ci in range(NPCH):
            buf = ci % 2
            src = i_xnt_fa if ci < 8 else i_xnt_fb
            gate = chunk_pe.get((1, ci - 2)) if ci >= 2 else chunk_pe[(0, NPCH - 1)]
            wait("sync", "pe", gate)
            coff = (ci % 8) * PCH
            dma("sync", sb_xnt[buf][:].rearrange("p (n m) -> p n m", m=PCH),
                src[:, coff:coff + PCH].rearrange("(n p) m -> p n m", p=128), "sq")
            v_ld = cur("sq")
            wait("tensor", "sq", v_ld)
            if first_v:
                wait("tensor", "ac", v_wvready)
                first_v = False
            else:
                wait("tensor", "dv", v_round_evac)
            for dt in range(4):
                for ht in range(HT):
                    op("tensor", lambda e, dt=dt, ht=ht, buf=buf: e.matmul(
                        PS4[dt],
                        sb_w1[:, ht * 1024 + dt * 128: ht * 1024 + dt * 128 + 128],
                        sb_xnt[buf][:, ht * PCH:(ht + 1) * PCH],
                        start=(ht == 0), stop=(ht == HT - 1)),
                        "pe" if ht == HT - 1 else None)
            v_mm = cur("pe")
            wait("vector", "pe", v_mm)
            for dt in range(4):
                prev = vt_dv.get((ci - 1, "tp"))
                if prev:
                    wait("vector", "pe", prev)
                if ci == 0:
                    wait("vector", "sq", stg_last.get(dt, 0))
                op("vector", lambda e, dt=dt: e.tensor_scalar_add(
                    sb_pstg[dt][:], PS4[dt],
                    sb_bias[:, 8 + dt:9 + dt]), "dv")
            v_round_evac = cur("dv")
            wait("tensor", "dv", v_round_evac)
            for ts in range(4):
                prevdv = vt_dv.get((ci, ts - 1)) or vt_dv.get((ci - 1, 3))
                if prevdv:
                    wait("tensor", "dv", prevdv)
                for dt in range(4):
                    op("tensor", lambda e, dt=dt, ts=ts: e.transpose(
                        ps_t[:, dt * 128:(dt + 1) * 128],
                        sb_pstg[dt][:, ts * 128:(ts + 1) * 128], c_idb[:]),
                        "pe" if dt == 3 else None)
                if ts == 3:
                    vt_dv[(ci, "tp")] = cur("pe")
                wait("vector", "pe", cur("pe"))
                prevsq = vt_dv.get((ci, ts - 1, "dma")) or \
                    vt_dv.get((ci - 1, 3, "dma"))
                if prevsq:
                    wait("vector", "sq", prevsq)
                op("vector", lambda e: e.tensor_copy(sb_vtt[:], ps_t[:]), "dv")
                vt_dv[(ci, ts)] = cur("dv")
                wait("sync", "dv", vt_dv[(ci, ts)])
                dma("sync", i_vn[ci * 512 + ts * 128: ci * 512 + (ts + 1) * 128, :],
                    sb_vtt[:], "sq")
                vt_dv[(ci, ts, "dma")] = cur("sq")
            chunk_pe[(1, ci)] = cur("pe")
        v_p1 = dict(pe=cur("pe"), dv=cur("dv"), ac=cur("ac"), sq=cur("sq"),
                    gq=cur("gq"))
        p1.close()

        pD = ExitStack()
        sb_pk = pD.enter_context(nc.sbuf_tensor("sb_pk", [128, NIT * JH], U8))
        sb_e8 = pD.enter_context(nc.sbuf_tensor("sb_e8", [128, 512], U8))
        sb_eb = pD.enter_context(nc.sbuf_tensor("sb_eb", [128, 512], BF16))
        sb_sr = pD.enter_context(nc.sbuf_tensor("sb_sr", [128, IS], BF16))
        sb_zr = pD.enter_context(nc.sbuf_tensor("sb_zr", [128, IS], BF16))
        sb_t1 = pD.enter_context(nc.sbuf_tensor("sb_t1", [128, 512], BF16))
        sb_wh = [pD.enter_context(nc.sbuf_tensor(f"sb_wh{i}", [128, IS], BF16))
                 for i in range(2)]
        sb_row = pD.enter_context(nc.sbuf_tensor("sb_row", [1, 4 * IS], BF16))
        sb_zow = pD.enter_context(nc.sbuf_tensor("sb_zow", [1, 4 * IS], BF16))
        sb_dsr = pD.enter_context(nc.sbuf_tensor("sb_dsr", [128, D], BF16))
        sb_dzr = pD.enter_context(nc.sbuf_tensor("sb_dzr", [128, D], BF16))
        sb_drow = pD.enter_context(nc.sbuf_tensor("sb_drow", [1, 4 * D], BF16))
        sb_dzow = pD.enter_context(nc.sbuf_tensor("sb_dzow", [1, 4 * D], BF16))
        sb_dwh = [pD.enter_context(nc.sbuf_tensor(f"sb_dwh{i}", [128, D], BF16))
                  for i in range(2)]

        for s in ("pe", "dv", "ac", "sq"):
            wait("gpsimd", s, v_p1[s])

        wh_dma = {}
        repmm_pe = {}

        def deq_gateup(mi, d_pkm, d_sm, d_zm, i_wm):
            wait("gpsimd", "dv", cur("dv"))
            dma("gpsimd", sb_pk[:].rearrange("p (n m) -> p n m", m=JH),
                d_pkm.rearrange("(n p) m -> p n m", p=128), "gq")
            v_pk = cur("gq")
            wait("vector", "gq", v_pk)
            for jt in range(16):
                prev = repmm_pe.get((mi, jt - 1))
                if prev:
                    wait("gpsimd", "pe", prev)
                dma("gpsimd", sb_row[:],
                    d_sm[4 * jt:4 * jt + 4, :].rearrange("a b -> (a b)"), "gq")
                dma("gpsimd", sb_zow[:],
                    d_zm[4 * jt:4 * jt + 4, :].rearrange("a b -> (a b)"), "gq")
                v_rows = cur("gq")
                wait("tensor", "gq", v_rows)
                for (c0, cw) in ICHUNKS:
                    wait("tensor", "dv", cur("dv"))
                    for b in range(4):
                        op("tensor", lambda e, b=b, c0=c0, cw=cw: e.matmul(
                            ps_m[32 * b:32 * b + 32, 0:cw], c_o1b[0:1, 0:32],
                            sb_row[0:1, b * IS + c0:b * IS + c0 + cw],
                            start=True, stop=True,
                            tile_position=(0, 32 * b)), None)
                        op("tensor", lambda e, b=b, c0=c0, cw=cw: e.matmul(
                            ps_o[32 * b:32 * b + 32, 0:cw], c_o1b[0:1, 0:32],
                            sb_zow[0:1, b * IS + c0:b * IS + c0 + cw],
                            start=True, stop=True,
                            tile_position=(0, 32 * b)), "pe" if b == 3 else None)
                    wait("vector", "pe", cur("pe"))
                    op("vector", lambda e, c0=c0, cw=cw: e.tensor_copy(
                        sb_sr[:, c0:c0 + cw], ps_m[:, 0:cw]), None)
                    op("vector", lambda e, c0=c0, cw=cw: e.tensor_copy(
                        sb_zr[:, c0:c0 + cw], ps_o[:, 0:cw]), "dv")
                repmm_pe[(mi, jt)] = cur("pe")
                v_reps = cur("dv")
                for half in range(2):
                    prevd = wh_dma.get((mi, jt - 1, half)) if jt else \
                        wh_dma.get((mi - 1, 15, half))
                    if prevd:
                        wait("vector", "sq", prevd)
                    for (c0, cw) in ICHUNKS:
                        nits = cw // 128
                        for k in range(nits):
                            it = c0 // 128 + k
                            shift = (4, ALU.logical_shift_right) if half == 0 \
                                else (15, ALU.bitwise_and)
                            op("vector", lambda e, it=it, jt=jt, k=k, sh=shift:
                               e.tensor_single_scalar(
                                sb_e8[:, k * 128:(k + 1) * 128],
                                sb_pk[:, it * JH + jt * 128:
                                      it * JH + jt * 128 + 128],
                                sh[0], sh[1]), "dv")
                        op("vector", lambda e, cw=cw: e.tensor_copy(
                            sb_eb[:, 0:cw], sb_e8[:, 0:cw]), "dv")
                        wait("tensor", "dv", cur("dv"))
                        for k in range(nits):
                            op("tensor", lambda e, k=k: e.transpose(
                                ps_t[:, k * 128:(k + 1) * 128],
                                sb_eb[:, k * 128:(k + 1) * 128], c_idb[:]),
                                "pe" if k == nits - 1 else None)
                        wait("vector", "pe", cur("pe"))
                        op("vector", lambda e, c0=c0, cw=cw: e.tensor_sub(
                            sb_t1[:, 0:cw], ps_t[:, 0:cw], sb_zr[:, c0:c0 + cw]),
                            None)
                        op("vector", lambda e, half=half, c0=c0, cw=cw:
                           e.tensor_mul(
                            sb_wh[half][:, c0:c0 + cw], sb_t1[:, 0:cw],
                            sb_sr[:, c0:c0 + cw]), "dv")
                    wait("sync", "dv", cur("dv"))
                    roff = (0 if half == 0 else JH) + jt * 128
                    dma("sync", i_wm[roff:roff + 128, :], sb_wh[half][:], "sq")
                    wh_dma[(mi, jt, half)] = cur("sq")

        deq_gateup(0, d_gpk, d_gs, d_gz, i_wg)
        deq_gateup(1, d_upk, d_us, d_uz, i_wu)

        wait("gpsimd", "dv", cur("dv"))
        dma("gpsimd", sb_pk[:, 0:32 * JI].rearrange("p (n m) -> p n m", m=JI),
            d_dpk.rearrange("(n p) m -> p n m", p=128), "gq")
        v_dpk = cur("gq")
        wait("vector", "gq", v_dpk)
        dwh_dma = {}
        for jt in range(NJT_D):
            jw = 128 if jt < 5 else 64
            nb = jw // 32
            prev = repmm_pe.get(("d", jt - 1))
            if prev:
                wait("gpsimd", "pe", prev)
            dma("gpsimd", sb_drow[0:1, 0:nb * D],
                d_ds[4 * jt:4 * jt + nb, :].rearrange("a b -> (a b)"), "gq")
            dma("gpsimd", sb_dzow[0:1, 0:nb * D],
                d_dz[4 * jt:4 * jt + nb, :].rearrange("a b -> (a b)"), "gq")
            v_rows = cur("gq")
            wait("tensor", "gq", v_rows)
            for c0 in range(0, D, 512):
                wait("tensor", "dv", cur("dv"))
                for b in range(nb):
                    op("tensor", lambda e, b=b, c0=c0: e.matmul(
                        ps_m[32 * b:32 * b + 32, :], c_o1b[0:1, 0:32],
                        sb_drow[0:1, b * D + c0:b * D + c0 + 512],
                        start=True, stop=True, tile_position=(0, 32 * b)), None)
                    op("tensor", lambda e, b=b, c0=c0: e.matmul(
                        ps_o[32 * b:32 * b + 32, :], c_o1b[0:1, 0:32],
                        sb_dzow[0:1, b * D + c0:b * D + c0 + 512],
                        start=True, stop=True, tile_position=(0, 32 * b)),
                        "pe" if b == nb - 1 else None)
                wait("vector", "pe", cur("pe"))
                op("vector", lambda e, c0=c0, jw=jw: e.tensor_copy(
                    sb_dsr[0:jw, c0:c0 + 512], ps_m[0:jw, :]), None)
                op("vector", lambda e, c0=c0, jw=jw: e.tensor_copy(
                    sb_dzr[0:jw, c0:c0 + 512], ps_o[0:jw, :]), "dv")
            repmm_pe[("d", jt)] = cur("pe")
            for half in range(2):
                prevd = dwh_dma.get((jt - 1, half))
                if prevd:
                    wait("vector", "sq", prevd)
                for hc in range(8):
                    for k in range(4):
                        htile = hc * 4 + k
                        shift = (4, ALU.logical_shift_right) if half == 0 \
                            else (15, ALU.bitwise_and)
                        op("vector", lambda e, htile=htile, jt=jt, k=k, jw=jw,
                           sh=shift: e.tensor_single_scalar(
                            sb_e8[:, k * 128:k * 128 + jw],
                            sb_pk[:, htile * JI + jt * 128:
                                  htile * JI + jt * 128 + jw],
                            sh[0], sh[1]), "dv")
                    op("vector", lambda e: e.tensor_copy(sb_eb[:], sb_e8[:]),
                       "dv")
                    wait("tensor", "dv", cur("dv"))
                    for k in range(4):
                        op("tensor", lambda e, k=k, jw=jw: e.transpose(
                            ps_t[0:jw, k * 128:(k + 1) * 128],
                            sb_eb[:, k * 128:k * 128 + jw], c_idb[:]),
                            "pe" if k == 3 else None)
                    wait("vector", "pe", cur("pe"))
                    op("vector", lambda e, hc=hc, jw=jw: e.tensor_sub(
                        sb_t1[0:jw, :], ps_t[0:jw, 0:512],
                        sb_dzr[0:jw, hc * 512:(hc + 1) * 512]), None)
                    op("vector", lambda e, half=half, hc=hc, jw=jw: e.tensor_mul(
                        sb_dwh[half][0:jw, hc * 512:(hc + 1) * 512],
                        sb_t1[0:jw, 0:512],
                        sb_dsr[0:jw, hc * 512:(hc + 1) * 512]), "dv")
                wait("sync", "dv", cur("dv"))
                roff = (0 if half == 0 else JI) + jt * 128
                dma("sync", i_wd[roff:roff + jw, :], sb_dwh[half][0:jw, :], "sq")
                dwh_dma[(jt, half)] = cur("sq")
        v_pD = dict(pe=cur("pe"), dv=cur("dv"), ac=cur("ac"), sq=cur("sq"),
                    gq=cur("gq"))
        pD.close()

        p2 = ExitStack()
        sb_kT = p2.enter_context(nc.sbuf_tensor("sb_kT", [128, 4 * T], BF16))
        sb_vn = p2.enter_context(nc.sbuf_tensor("sb_vn", [128, 4 * T], BF16))
        sb_qt = [p2.enter_context(nc.sbuf_tensor(f"sb_qt{i}", [128, 512], BF16))
                 for i in range(2)]
        sb_q2 = p2.enter_context(nc.sbuf_tensor("sb_q2", [128, 512], BF16))
        sb_pt = [p2.enter_context(nc.sbuf_tensor(f"sb_pt{i}", [128, 1024], BF16))
                 for i in range(2)]
        sb_rep = p2.enter_context(nc.sbuf_tensor("sb_rep", [128, 512], F32))
        sb_at = [p2.enter_context(nc.sbuf_tensor(f"sb_at{i}", [128, 512], BF16))
                 for i in range(2)]
        sb_mk = p2.enter_context(nc.sbuf_tensor("sb_mk", [128, 4 * ACH], BF16))
        sb_nb = p2.enter_context(nc.sbuf_tensor("sb_nb", [128, 1], F32))
        sb_mk2 = p2.enter_context(nc.sbuf_tensor("sb_mk2", [1, 4], F32))
        sb_k2 = p2.enter_context(nc.sbuf_tensor("sb_k2", [128, 512], BF16))
        sb_m16 = p2.enter_context(nc.sbuf_tensor("sb_m16", [1, 16], F32))
        sb_bq = p2.enter_context(nc.sbuf_tensor("sb_bq", [1, 512], F32))
        sb_bm = p2.enter_context(nc.sbuf_tensor("sb_bm", [1, 1], F32))
        sb_rc = p2.enter_context(nc.sbuf_tensor("sb_rc", [1, 512], F32))

        for s in ("pe", "dv", "ac", "gq"):
            wait("sync", s, v_pD[s])
        dma("sync", sb_mk[:], d_masks, "sq")
        for h in range(4):
            dma("sync", sb_kT[:, h * T:(h + 1) * T],
                i_kt[h * 128:(h + 1) * 128, :], "sq")
            dma("sync", sb_vn[:, h * T:(h + 1) * T].rearrange(
                "p (n m) -> p n m", m=128),
                i_vn[:, h * 128:(h + 1) * 128].rearrange("(n p) m -> p n m",
                                                         p=128), "sq")
        v_kv = cur("sq")

        wait("scalar", "sq", v_kv)
        wait("tensor", "sq", v_kv)
        for h in range(4):
            for chk in range(16):
                wait("scalar", "dv", cur("dv"))
                op("scalar", lambda e, h=h, chk=chk: e.activation(
                    sb_k2[:], sb_kT[:, h * T + chk * 512: h * T + (chk + 1) * 512],
                    AF.Square), "ac")
                wait("tensor", "ac", cur("ac"))
                wait("tensor", "dv", cur("dv"))
                op("tensor", lambda e: e.matmul(
                    ps_s[:], c_ocb[:], sb_k2[:], start=True, stop=True), "pe")
                wait("vector", "pe", cur("pe"))
                op("vector", lambda e, chk=chk: e.reduce_max(
                    sb_m16[0:1, chk:chk + 1], ps_s[0:1, :], axis=AX.X), "dv")
            op("vector", lambda e: e.engine_nop(), None)
            op("vector", lambda e: e.engine_nop(), None)
            op("vector", lambda e, h=h: e.reduce_max(
                sb_mk2[0:1, h:h + 1], sb_m16[:], axis=AX.X), "dv")
        wait("scalar", "dv", cur("dv"))
        op("scalar", lambda e: e.mul(sb_mk2[:], sb_mk2[:], (SCALE * 1.05) ** 2),
           "ac")
        v_mk2 = cur("ac")
        wait("tensor", "ac", v_mk2)

        att_pe = {}
        exp_ac = {}
        pair_pv = {}
        at_dma = {}
        epi_dv = {}
        for h in range(4):
            for ci in range(NACH):
                cid = h * NACH + ci
                par = cid % 2
                nk = 4 * (ci + 1)
                if cid >= 2:
                    wait("sync", "pe", att_pe[cid - 2])
                dma("sync", sb_qt[par][:],
                    i_qt[h * 128:(h + 1) * 128, ci * ACH:(ci + 1) * ACH], "sq")
                v_qld = cur("sq")
                wait("scalar", "sq", v_qld)
                if cid >= 1:
                    wait("scalar", "dv", epi_dv[cid - 1])
                op("scalar", lambda e, par=par: e.activation(
                    sb_q2[:], sb_qt[par][:], AF.Square), "ac")
                wait("tensor", "ac", cur("ac"))
                if cid >= 1:
                    wait("tensor", "dv", epi_dv[cid - 1])
                op("tensor", lambda e: e.matmul(
                    ps_s[:], c_ocb[:], sb_q2[:], start=True, stop=True), "pe")
                wait("scalar", "pe", cur("pe"))
                op("scalar", lambda e, h=h: e.activation(
                    sb_bq[:], ps_s[:], AF.Sqrt, scale=sb_mk2[0:1, h:h + 1]), "ac")
                wait("vector", "ac", cur("ac"))
                op("vector", lambda e: e.reduce_max(sb_bm[:], sb_bq[:], axis=AX.X),
                   "dv")
                op("vector", lambda e: e.engine_nop(), None)
                op("vector", lambda e: e.engine_nop(), None)
                op("vector", lambda e: e.tensor_single_scalar(
                    sb_bm[:], sb_bm[:], -1.0, ALU.mult), "dv")
                wait("tensor", "dv", cur("dv"))
                op("tensor", lambda e: e.matmul(
                    ps_m[:, 0:1], c_o1f[:], sb_bm[:], start=True, stop=True), "pe")
                wait("vector", "pe", cur("pe"))
                op("vector", lambda e: e.tensor_copy(sb_nb[:], ps_m[:, 0:1]), "dv")
                v_nb = cur("dv")

                npair = nk // 2
                for p in range(npair):
                    kt0, kt1 = 2 * p, 2 * p + 1
                    pp = p % 2
                    psS = ps_a if pp == 0 else ps_b
                    key = (cid, pp)
                    prev_exp = exp_ac.get((cid, p - 2))
                    if prev_exp:
                        wait("tensor", "ac", prev_exp)
                    for kk, kt in ((0, kt0), (1, kt1)):
                        if kt == 0:
                            wait("tensor", "dv", v_nb)
                            if cid >= 1:
                                wait("tensor", "dv", epi_dv[cid - 1])
                        op("tensor", lambda e, kk=kk, kt=kt, h=h, par=par,
                           psS=psS: e.matmul(
                            psS[:, kk * 512:(kk + 1) * 512],
                            sb_kT[:, h * T + kt * 128: h * T + kt * 128 + 128],
                            sb_qt[par][:], start=True, stop=True),
                            "pe" if kk == 1 else None)
                    v_st = cur("pe")
                    wait("scalar", "pe", v_st)
                    wait("scalar", "dv", v_nb)
                    prev_pv = pair_pv.get((cid, p - 2))
                    if prev_pv:
                        wait("scalar", "pe", prev_pv)
                    exp_ac[(cid, p)] = op(
                        "scalar", lambda e, pp=pp, psS=psS: e.activation(
                            sb_pt[pp][:], psS[:], AF.Exp, bias=sb_nb[:],
                            scale=SCALE), "ac")
                    diag0 = nk - 4
                    masked = False
                    for kk, kt in ((0, kt0), (1, kt1)):
                        if kt >= diag0:
                            ms = kt - diag0
                            wait("vector", "ac", exp_ac[(cid, p)])
                            op("vector", lambda e, pp=pp, kk=kk, ms=ms:
                               e.tensor_mul(
                                sb_pt[pp][:, kk * 512:(kk + 1) * 512],
                                sb_pt[pp][:, kk * 512:(kk + 1) * 512],
                                sb_mk[:, ms * 512:(ms + 1) * 512]), "dv")
                            masked = True
                    if masked:
                        wait("tensor", "dv", cur("dv"))
                    else:
                        wait("tensor", "ac", exp_ac[(cid, p)])
                    for kk, kt in ((0, kt0), (1, kt1)):
                        op("tensor", lambda e, kk=kk, kt=kt, h=h, pp=pp: e.matmul(
                            ps_o[:],
                            sb_vn[:, h * T + kt * 128: h * T + kt * 128 + 128],
                            sb_pt[pp][:, kk * 512:(kk + 1) * 512],
                            start=(kt == 0), stop=(kt == nk - 1)), None)
                        op("tensor", lambda e, kk=kk, kt=kt, pp=pp: e.matmul(
                            ps_s[:], c_ocb[:],
                            sb_pt[pp][:, kk * 512:(kk + 1) * 512],
                            start=(kt == 0), stop=(kt == nk - 1)),
                            "pe" if kk == 1 else None)
                    pair_pv[(cid, p)] = cur("pe")
                att_pe[cid] = cur("pe")
                wait("vector", "pe", att_pe[cid])
                op("vector", lambda e: e.reciprocal(sb_rc[:], ps_s[:]), "dv")
                op("vector", lambda e: e.engine_nop(), None)
                op("vector", lambda e: e.engine_nop(), "dv")
                wait("tensor", "dv", cur("dv"))
                op("tensor", lambda e: e.matmul(
                    ps_m[:], c_o1f[:], sb_rc[:], start=True, stop=True), "pe")
                wait("vector", "pe", cur("pe"))
                op("vector", lambda e: e.tensor_copy(sb_rep[:], ps_m[:]), "dv")
                if cid >= 2:
                    wait("vector", "sq", at_dma[cid - 2])
                op("vector", lambda e, par=par: e.tensor_mul(
                    sb_at[par][:], ps_o[:], sb_rep[:]), "dv")
                epi_dv[cid] = cur("dv")
                wait("sync", "dv", epi_dv[cid])
                dma("sync", i_at[h * 128:(h + 1) * 128, ci * ACH:(ci + 1) * ACH],
                    sb_at[par][:], "sq")
                at_dma[cid] = cur("sq")
        v_p2 = dict(pe=cur("pe"), dv=cur("dv"), ac=cur("ac"), sq=cur("sq"),
                    gq=cur("gq"))
        p2.close()

        p3 = ExitStack()
        sb_xt3 = [p3.enter_context(nc.sbuf_tensor(f"sb_xt3{i}", [128, 512], F32))
                  for i in range(2)]
        sb_at3 = [p3.enter_context(nc.sbuf_tensor(f"sb_at3{i}", [128, 512], BF16))
                  for i in range(2)]
        sb_r1 = [p3.enter_context(nc.sbuf_tensor(f"sb_r13{i}", [128, 512], F32))
                 for i in range(2)]
        sb_r1b = p3.enter_context(nc.sbuf_tensor("sb_r1b", [128, 512], BF16))
        sb_sq3 = p3.enter_context(nc.sbuf_tensor("sb_sq3", [128, 512], BF16))
        sb_s1 = p3.enter_context(nc.sbuf_tensor("sb_s1", [1, T], F32))
        sb_s2 = p3.enter_context(nc.sbuf_tensor("sb_s2", [1, T], F32))
        sb_mean = p3.enter_context(nc.sbuf_tensor("sb_mean", [1, T], F32))
        sb_rstd = p3.enter_context(nc.sbuf_tensor("sb_rstd", [1, T], F32))
        sb_tq = p3.enter_context(nc.sbuf_tensor("sb_tq", [1, T], F32))
        sb_mrep = p3.enter_context(nc.sbuf_tensor("sb_mrep", [128, 512], F32))
        sb_rrep = p3.enter_context(nc.sbuf_tensor("sb_rrep", [128, 512], F32))
        sb_x2o = [p3.enter_context(nc.sbuf_tensor(f"sb_x2o{i}", [128, 512], BF16))
                  for i in range(2)]

        for s in ("pe", "dv", "ac"):
            wait("sync", s, v_p2[s])

        r1cons = {}
        st = 0
        for ci in range(16):
            for ht in range(4):
                par = st % 2
                if st >= 2:
                    wait("sync", "dv", r1cons[st - 2])
                dma("sync", sb_xt3[par][:],
                    i_xt[ht * 128:(ht + 1) * 128, ci * 512:(ci + 1) * 512], "sq")
                dma("sync", sb_at3[par][:],
                    i_at[ht * 128:(ht + 1) * 128, ci * 512:(ci + 1) * 512], "sq")
                v_ld = cur("sq")
                wait("vector", "sq", v_ld)
                if st >= 1:
                    wait("vector", "pe", cur("pe"))
                op("vector", lambda e, par=par: e.tensor_add(
                    sb_r1[par][:], sb_xt3[par][:], sb_at3[par][:]), None)
                op("vector", lambda e, par=par: e.tensor_copy(
                    sb_r1b[:], sb_r1[par][:]), "dv")
                v_r1 = cur("dv")
                r1cons[st] = v_r1
                wait("sync", "dv", v_r1)
                dma("sync", i_r1[ht * 128:(ht + 1) * 128, ci * 512:(ci + 1) * 512],
                    sb_r1[par][:], "sq")
                wait("scalar", "dv", v_r1)
                if st >= 1:
                    wait("scalar", "pe", cur("pe"))
                op("scalar", lambda e: e.activation(sb_sq3[:], sb_r1b[:],
                                                    AF.Square), "ac")
                wait("tensor", "dv", v_r1)
                wait("tensor", "ac", cur("ac"))
                op("tensor", lambda e, ht=ht: e.matmul(
                    ps_s[:], c_ocb[:], sb_r1b[:],
                    start=(ht == 0), stop=(ht == 3)), None)
                op("tensor", lambda e, ht=ht: e.matmul(
                    ps_m[0:1, :], c_ocb[:], sb_sq3[:],
                    start=(ht == 0), stop=(ht == 3)), "pe")
                st += 1
            wait("vector", "pe", cur("pe"))
            op("vector", lambda e, ci=ci: e.tensor_copy(
                sb_s1[0:1, ci * 512:(ci + 1) * 512], ps_s[:]), None)
            op("vector", lambda e, ci=ci: e.tensor_copy(
                sb_s2[0:1, ci * 512:(ci + 1) * 512], ps_m[0:1, :]), "dv")
            wait("tensor", "dv", cur("dv"))
        wait("sync", "dv", cur("dv"))
        dma("sync", i_ln2a[0:1, :], sb_s1[:], "sq")
        dma("sync", i_ln2a[1:2, :], sb_s2[:], "sq")
        wait("gpsimd", "sq", cur("sq"))
        v_ar2 = cc("AllReduce", ALU.add, i_ln2a, i_ln2b)
        wait("sync", "cc", v_ar2)
        dma("sync", sb_mean[:], i_ln2b[0:1, :], "sq")
        dma("sync", sb_rstd[:], i_ln2b[1:2, :], "sq")
        v_mr = cur("sq")
        wait("vector", "sq", v_mr)
        op("vector", lambda e: e.tensor_single_scalar(
            sb_mean[:], sb_mean[:], 1.0 / D, ALU.mult), None)
        op("vector", lambda e: e.tensor_single_scalar(
            sb_rstd[:], sb_rstd[:], 1.0 / D, ALU.mult), "dv")
        wait("scalar", "dv", cur("dv"))
        op("scalar", lambda e: e.activation(sb_tq[:], sb_mean[:],
                                            AF.Square), "ac")
        wait("vector", "ac", cur("ac"))
        op("vector", lambda e: e.tensor_sub(sb_rstd[:], sb_rstd[:],
                                            sb_tq[:]), "dv")
        wait("scalar", "dv", cur("dv"))
        op("scalar", lambda e: e.activation(sb_tq[:], sb_rstd[:],
                                            AF.Sqrt, bias=c_eps[0:1, :]), "ac")
        wait("vector", "ac", cur("ac"))
        op("vector", lambda e: e.reciprocal(sb_rstd[:], sb_tq[:]), "dv")
        op("vector", lambda e: e.engine_nop(), None)
        op("vector", lambda e: e.engine_nop(), "dv")
        v_ln2 = cur("dv")
        wait("tensor", "dv", v_ln2)

        x2cons = {}
        x2dma = {}
        st = 0
        prev_muls = 0
        for ci in range(16):
            if prev_muls:
                wait("tensor", "dv", prev_muls)
            op("tensor", lambda e, ci=ci: e.matmul(
                ps_m[:], c_o1f[:], sb_mean[0:1, ci * 512:(ci + 1) * 512],
                start=True, stop=True), None)
            op("tensor", lambda e, ci=ci: e.matmul(
                ps_o[:], c_o1f[:], sb_rstd[0:1, ci * 512:(ci + 1) * 512],
                start=True, stop=True), "pe")
            wait("vector", "pe", cur("pe"))
            if ci >= 1:
                pass
            op("vector", lambda e: e.tensor_copy(sb_mrep[:], ps_m[:]), None)
            op("vector", lambda e: e.tensor_copy(sb_rrep[:], ps_o[:]), "dv")
            v_rep = cur("dv")
            for ht in range(4):
                par = st % 2
                if st >= 2:
                    wait("sync", "dv", x2cons[st - 2])
                dma("sync", sb_xt3[par][:],
                    i_r1[ht * 128:(ht + 1) * 128, ci * 512:(ci + 1) * 512], "sq")
                v_ld = cur("sq")
                wait("vector", "sq", v_ld)
                if st >= 2:
                    wait("vector", "sq", x2dma[st - 2])
                op("vector", lambda e, par=par: e.tensor_sub(
                    sb_r1[par][:], sb_xt3[par][:], sb_mrep[:]), None)
                op("vector", lambda e, par=par: e.tensor_mul(
                    sb_x2o[par][:], sb_r1[par][:], sb_rrep[:]), "dv")
                x2cons[st] = cur("dv")
                wait("sync", "dv", x2cons[st])
                half = i_x2t_a if ci < 8 else i_x2t_b
                coff = (ci % 8) * 512
                dma("sync", half[ht * 128:(ht + 1) * 128, coff:coff + 512],
                    sb_x2o[par][:], "sq")
                x2dma[st] = cur("sq")
                st += 1
            prev_muls = cur("dv")
        wait("gpsimd", "sq", cur("sq"))
        wait("gpsimd", "cc", v_ar2)
        v_ag2a = cc("AllGather", ALU.bypass, i_x2t_a, i_x2t_fa)
        wait("gpsimd", "cc", v_ag2a)
        wait("gpsimd", "sq", cur("sq"))
        v_ag2b = cc("AllGather", ALU.bypass, i_x2t_b, i_x2t_fb)
        v_p3 = dict(pe=cur("pe"), dv=cur("dv"), ac=cur("ac"), sq=cur("sq"),
                    gq=cur("gq"))
        p3.close()

        p4 = ExitStack()
        sb_x2 = p4.enter_context(nc.sbuf_tensor("sb_x2", [128, HT * FBLK], BF16))
        sb_wgu = [p4.enter_context(nc.sbuf_tensor(f"sb_wgu{i}", [128, HT * 128],
                                                  BF16)) for i in range(4)]
        sb_sg = p4.enter_context(nc.sbuf_tensor("sb_sg", [128, 512], BF16))
        sb_hs = [p4.enter_context(nc.sbuf_tensor(f"sb_hs{i}", [128, 512], BF16))
                 for i in range(2)]

        for s in ("pe", "dv", "ac"):
            wait("sync", s, v_p3[s])
            wait("gpsimd", s, v_p3[s])
        wait("gpsimd", "sq", v_p3["sq"])

        hs_dma = {}
        wcons = []
        gu_pe = {}
        kix = 0
        for blk in range(NFBLK):
            wait("sync", "cc", v_ag2a if blk < 2 else v_ag2b)
            if blk >= 1:
                wait("sync", "pe", cur("pe"))
            src = i_x2t_fa if blk < 2 else i_x2t_fb
            coff = (blk % 2) * FBLK
            dma("sync", sb_x2[:, 0:16 * FBLK].rearrange("p (n m) -> p n m",
                                                        m=FBLK),
                src[0:D:2, coff:coff + FBLK].rearrange("(n p) m -> p n m", p=128),
                "sq")
            dma("sync", sb_x2[:, 16 * FBLK:].rearrange("p (n m) -> p n m",
                                                       m=FBLK),
                src[1:D:2, coff:coff + FBLK].rearrange("(n p) m -> p n m", p=128),
                "sq")
            v_x2ld = cur("sq")
            for it in range(NIT):
                gbuf = ((blk * NIT + it) % 2) * 2
                if len(wcons) >= 2:
                    wait("gpsimd", "pe", wcons[-2])
                for mi, i_wm in ((0, i_wg), (1, i_wu)):
                    dma("gpsimd",
                        sb_wgu[gbuf + mi][:].rearrange(
                            "p (n m) -> p n m", m=128),
                        i_wm[:, it * 128:(it + 1) * 128].rearrange(
                            "(n p) m -> p n m", p=128), "gq")
                v_wld = cur("gq")
                wait("tensor", "gq", v_wld)
                wait("tensor", "sq", v_x2ld)
                for c in range(4):
                    pp = kix % 2
                    psG = ps_a if pp == 0 else ps_b
                    prev = gu_pe.get(kix - 2)
                    if prev is not None:
                        wait("tensor", "ac", prev[0])
                        wait("tensor", "dv", prev[1])
                    for ht in range(HT):
                        op("tensor", lambda e, ht=ht, c=c, g=gbuf, psG=psG:
                           e.matmul(
                            psG[:, 0:512],
                            sb_wgu[g][:, ht * 128:(ht + 1) * 128],
                            sb_x2[:, ht * FBLK + c * 512:
                                  ht * FBLK + (c + 1) * 512],
                            start=(ht == 0), stop=(ht == HT - 1)), None)
                    for ht in range(HT):
                        op("tensor", lambda e, ht=ht, c=c, g=gbuf, psG=psG:
                           e.matmul(
                            psG[:, 512:1024],
                            sb_wgu[g + 1][:, ht * 128:(ht + 1) * 128],
                            sb_x2[:, ht * FBLK + c * 512:
                                  ht * FBLK + (c + 1) * 512],
                            start=(ht == 0), stop=(ht == HT - 1)),
                            "pe" if ht == HT - 1 else None)
                    v_gu = cur("pe")
                    wait("scalar", "pe", v_gu)
                    if kix >= 1:
                        wait("scalar", "dv", cur("dv"))
                    v_silu = op("scalar", lambda e, psG=psG: e.activation(
                        sb_sg[:], psG[:, 0:512], AF.Silu), "ac")
                    wait("vector", "ac", v_silu)
                    hpar = kix % 2
                    if kix >= 2:
                        wait("vector", "sq", hs_dma[kix - 2])
                    op("vector", lambda e, psG=psG, hpar=hpar: e.tensor_mul(
                        sb_hs[hpar][:], sb_sg[:], psG[:, 512:1024]), "dv")
                    gu_pe[kix] = (v_silu, cur("dv"))
                    wait("sync", "dv", cur("dv"))
                    dma("sync", i_hs[blk * 4 + c,
                                     it * 128:(it + 1) * 128, :],
                        sb_hs[hpar][:], "sq")
                    hs_dma[kix] = cur("sq")
                    kix += 1
                wcons.append(cur("pe"))
        v_p4 = dict(pe=cur("pe"), dv=cur("dv"), ac=cur("ac"), sq=cur("sq"),
                    gq=cur("gq"))
        p4.close()

        p5 = ExitStack()
        sb_wd = p5.enter_context(nc.sbuf_tensor("sb_wd", [128, NIT * D], BF16))
        sb_hc = [p5.enter_context(nc.sbuf_tensor(f"sb_hc{i}", [128, NIT * 512],
                                                 BF16)) for i in range(2)]
        sb_pd = [p5.enter_context(nc.sbuf_tensor(f"sb_pd{i}", [128, 512], BF16))
                 for i in range(2)]

        for s in ("pe", "dv", "ac", "sq"):
            wait("gpsimd", s, v_p4[s])
        for s in ("pe", "dv", "ac"):
            wait("sync", s, v_p4[s])
        dma("gpsimd", sb_wd[:].rearrange("p (n m) -> p n m", m=D),
            i_wd.rearrange("(n p) m -> p n m", p=128), "gq")
        v_wd = cur("gq")

        pd_dma = {}
        down_pe = {}
        dcons = {}
        for c in range(16):
            par = c % 2
            if c >= 2:
                wait("sync", "pe", down_pe[c - 2])
            dma("sync", sb_hc[par][:].rearrange("p (n m) -> p n m", m=512),
                i_hs[c].rearrange("(n p) m -> p n m", p=128), "sq")
            v_hld = cur("sq")
            wait("tensor", "sq", v_hld)
            if c == 0:
                wait("tensor", "gq", v_wd)
                wait("tensor", "ac", v_p4["ac"])
                wait("tensor", "dv", v_p4["dv"])
            for ht in range(HT):
                key = c * HT + ht
                PSD = (ps_o[:], ps_m[:], ps_a[:, 0:512], ps_b[:, 0:512])
                psD = PSD[key % 4]
                prev = dcons.get(key - 4)
                if prev:
                    wait("tensor", "dv", prev)
                for it in range(NIT):
                    op("tensor", lambda e, ht=ht, it=it, par=par, psD=psD:
                       e.matmul(
                        psD[:],
                        sb_wd[:, it * D + ht * 128: it * D + ht * 128 + 128],
                        sb_hc[par][:, it * 512:(it + 1) * 512],
                        start=(it == 0), stop=(it == NIT - 1)),
                        "pe" if it == NIT - 1 else None)
                v_d = cur("pe")
                wait("vector", "pe", v_d)
                if key >= 2:
                    wait("vector", "sq", pd_dma[key - 2])
                op("vector", lambda e, psD=psD, par2=key % 2: e.tensor_copy(
                    sb_pd[par2][:], psD[:]), "dv")
                dcons[key] = cur("dv")
                wait("sync", "dv", dcons[key])
                dma("sync", i_fp[ht * 128:(ht + 1) * 128, c * 512:(c + 1) * 512],
                    sb_pd[key % 2][:], "sq")
                pd_dma[key] = cur("sq")
            down_pe[c] = cur("pe")
        wait("gpsimd", "sq", cur("sq"))
        v_rs0 = cc("ReduceScatter", ALU.add, i_fp, i_fo)
        wait("gpsimd", "cc", v_rs0)
        v_b1 = cc("AllReduce", ALU.add, i_bar1, i_bar2)
        wait("gpsimd", "cc", v_b1)
        dma("gpsimd", i_foshadow, i_fo, "gq")
        wait("gpsimd", "gq", cur("gq"))
        v_rs = cc("AllReduce", ALU.add, i_bar1, i_bar2)
        wait("gpsimd", "cc", v_rs)
        dma("gpsimd", i_foshadow, i_fo, "gq")
        v_fosh2 = cur("gq")
        v_p5 = dict(pe=cur("pe"), dv=cur("dv"), ac=cur("ac"), sq=cur("sq"),
                    gq=cur("gq"))
        p5.close()

        p6 = ExitStack()
        sb_r16 = [p6.enter_context(nc.sbuf_tensor(f"sb_r16{i}", [128, 512], F32))
                  for i in range(2)]
        sb_fo = [p6.enter_context(nc.sbuf_tensor(f"sb_fo{i}", [128, 512], BF16))
                 for i in range(2)]
        sb_ot4 = p6.enter_context(nc.sbuf_tensor("sb_ot4", [128, 4 * 512], F32))
        sb_oo = [p6.enter_context(nc.sbuf_tensor(f"sb_oo{i}", [128, 512], F32))
                 for i in range(2)]

        wait("sync", "cc", v_rs)
        wait("sync", "gq", v_fosh2)
        for s in ("pe", "dv"):
            wait("sync", s, v_p5[s])
        if DEBUG:
            dma("sync", dbg["fo_early"], i_fo, "sq")
        osum = {}
        oo_dma = {}
        tp_pe = {}

        def p6_barrier():
            for eng in ("sync", "tensor", "vector", "scalar"):
                for s in ("pe", "dv", "ac", "sq"):
                    wait(eng, s, cur(s))

        for ci in range(16):
            p6_barrier()
            for ht in range(4):
                par = ht % 2
                dma("sync", sb_r16[par][:],
                    i_r1[ht * 128:(ht + 1) * 128, ci * 512:(ci + 1) * 512], "sq")
                dma("sync", sb_fo[par][:],
                    i_fo[ht * 128:(ht + 1) * 128, ci * 512:(ci + 1) * 512], "sq")
                v_ld = cur("sq")
                wait("vector", "sq", v_ld)
                op("vector", lambda e, ht=ht, par=par: e.tensor_add(
                    sb_ot4[:, ht * 512:(ht + 1) * 512], sb_r16[par][:],
                    sb_fo[par][:]), "dv")
                osum[(ci, ht)] = cur("dv")
                wait("sync", "dv", cur("dv"))
            p6_barrier()
            for ts in range(4):
                spar = ts % 2
                for ht in range(4):
                    op("tensor", lambda e, ht=ht, ts=ts: e.transpose(
                        ps_m[:, ht * 128:(ht + 1) * 128],
                        sb_ot4[:, ht * 512 + ts * 128: ht * 512 + ts * 128 + 128],
                        c_idf[:]), "pe" if ht == 3 else None)
                tp_pe[ci] = cur("pe")
                wait("scalar", "pe", cur("pe"))
                op("scalar", lambda e, spar=spar: e.copy(sb_oo[spar][:], ps_m[:]),
                   "ac")
                wait("sync", "ac", cur("ac"))
                dma("sync", d_out[ci * 512 + ts * 128: ci * 512 + (ts + 1) * 128,
                                  :], sb_oo[spar][:], "sq")
                oo_dma[(ci, ts)] = cur("sq")
                wait("tensor", "ac", cur("ac"))
                wait("scalar", "sq", cur("sq"))
        if DEBUG:
            for nm, sap in [("xnt_fa", i_xnt_fa), ("xnt_b", i_xnt_b),
                            ("xnt_fb", i_xnt_fb), ("qt", i_qt), ("kt", i_kt),
                            ("vn", i_vn), ("at", i_at), ("r1", i_r1),
                            ("x2t_fa", i_x2t_fa), ("wg", i_wg), ("wu", i_wu),
                            ("wd", i_wd),
                            ("hs", i_hs.rearrange("a b c -> (a b) c")),
                            ("fp", i_fp), ("fo", i_fo), ("fosh", i_foshadow)]:
                dma("sync", dbg[nm], sap, "sq")
        wait("sync", "sq", cur("sq"))
        p6.close()

        block = ctx.enter_context(nc.Block())

        def runner(name):
            def body(e):
                for fn in Q[name]:
                    fn(e)
            return body

        block.sync(runner("sync"))
        block.gpsimd(runner("gpsimd"))
        block.tensor(runner("tensor"))
        block.vector(runner("vector"))
        block.scalar(runner("scalar"))

    return nc


_NC_CACHE = {}


def _build_cached():
    if "nc" not in _NC_CACHE:
        _NC_CACHE["nc"] = build_nc()
    return _NC_CACHE["nc"]


def _byte_extract(a_int32):
    a = np.ascontiguousarray(a_int32)
    return np.ascontiguousarray(a.view(np.uint8).reshape(*a.shape, 4)[..., 0])


def kernel(L0_attn_x, L0_attn_Wq, L0_attn_bq, L0_attn_Wk, L0_attn_bk,
           L0_attn_Wv, L0_attn_bv, L0_ffn_gate_Wq, L0_ffn_up_Wq,
           L0_ffn_down_Wq, L0_ffn_gate_scale, L0_ffn_gate_zero,
           L0_ffn_up_scale, L0_ffn_up_zero, L0_ffn_down_scale,
           L0_ffn_down_zero):
    x = np.ascontiguousarray(np.asarray(L0_attn_x, dtype=np.float32)
                             .reshape(T, D))
    wq = np.asarray(L0_attn_Wq, dtype=np.float32)
    bq = np.asarray(L0_attn_bq, dtype=np.float32).reshape(D, 1)
    wk = np.asarray(L0_attn_Wk, dtype=np.float32)
    bk = np.asarray(L0_attn_bk, dtype=np.float32).reshape(2048, 1)
    wv = np.asarray(L0_attn_Wv, dtype=np.float32)
    bv = np.asarray(L0_attn_bv, dtype=np.float32).reshape(2048, 1)

    gpk = _byte_extract(np.asarray(L0_ffn_gate_Wq))
    upk = _byte_extract(np.asarray(L0_ffn_up_Wq))
    dpk = _byte_extract(np.asarray(L0_ffn_down_Wq))
    gpk_p = np.zeros((IPAD, JH), np.uint8); gpk_p[:11008] = gpk
    upk_p = np.zeros((IPAD, JH), np.uint8); upk_p[:11008] = upk
    dpk_p = np.zeros((D, IPAD // 2), np.uint8); dpk_p[:, :5504] = dpk
    gs_p = np.zeros((IPAD, 64), np.float32)
    gs_p[:11008] = np.asarray(L0_ffn_gate_scale, dtype=np.float32)
    gz_p = np.zeros((IPAD, 64), np.float32)
    gz_p[:11008] = np.asarray(L0_ffn_gate_zero, dtype=np.float32)
    us_p = np.zeros((IPAD, 64), np.float32)
    us_p[:11008] = np.asarray(L0_ffn_up_scale, dtype=np.float32)
    uz_p = np.zeros((IPAD, 64), np.float32)
    uz_p[:11008] = np.asarray(L0_ffn_up_zero, dtype=np.float32)
    ds_p = np.zeros((D, IPAD // 64), np.float32)
    ds_p[:, :172] = np.asarray(L0_ffn_down_scale, dtype=np.float32)
    dz_p = np.zeros((D, IPAD // 64), np.float32)
    dz_p[:, :172] = np.asarray(L0_ffn_down_zero, dtype=np.float32)

    identb = np.eye(128, dtype=ml_dtypes.bfloat16)
    identf = np.eye(128, dtype=np.float32)
    ones1b = np.ones((1, 128), dtype=ml_dtypes.bfloat16)
    ones1f = np.ones((1, 128), dtype=np.float32)
    onescb = np.ones((128, 1), dtype=ml_dtypes.bfloat16)
    masks = np.zeros((128, 4 * ACH), dtype=ml_dtypes.bfloat16)
    qf = np.arange(ACH)[None, :]
    kp = np.arange(128)[:, None]
    for p in range(4):
        masks[:, p * ACH:(p + 1) * ACH] = (qf >= 128 * p + kp).astype(
            ml_dtypes.bfloat16)

    in_maps = []
    for c in range(NCORES):
        fsl = slice(FS * c, FS * (c + 1))
        kvh = [(4 * c + i) % 16 for i in range(4)]
        kv_rows = np.concatenate([np.arange(128 * k, 128 * k + 128)
                                  for k in kvh])
        isl = slice(IS * c, IS * (c + 1))
        idx = np.r_[IS * c:IS * (c + 1):2, IS * c + 1:IS * (c + 1):2]
        in_maps.append(dict(
            xcol=np.ascontiguousarray(x[:, fsl]),
            wq=np.ascontiguousarray(wq[fsl, :]),
            bq=np.ascontiguousarray(bq[fsl, :]),
            wk=np.ascontiguousarray(wk[kv_rows, :]),
            bk=np.ascontiguousarray(bk[kv_rows, :]),
            wv=np.ascontiguousarray(wv[kv_rows, :]),
            bv=np.ascontiguousarray(bv[kv_rows, :]),
            gpk=np.ascontiguousarray(gpk_p[idx, :]),
            upk=np.ascontiguousarray(upk_p[idx, :]),
            dpk=np.ascontiguousarray(dpk_p[:, IS * c // 2:IS * (c + 1) // 2]),
            gsT=np.ascontiguousarray(gs_p[idx, :].T),
            gzT=np.ascontiguousarray(gz_p[idx, :].T),
            usT=np.ascontiguousarray(us_p[idx, :].T),
            uzT=np.ascontiguousarray(uz_p[idx, :].T),
            dsT=np.ascontiguousarray(
                np.pad(ds_p[:, IS * c // 64:IS * (c + 1) // 64].T,
                       ((0, 24 - IS // 64), (0, 0)))),
            dzT=np.ascontiguousarray(
                np.pad(dz_p[:, IS * c // 64:IS * (c + 1) // 64].T,
                       ((0, 24 - IS // 64), (0, 0)))),
            identb=identb, identf=identf, ones1b=ones1b, ones1f=ones1f,
            onescb=onescb,
            masks=masks,
        ))

    nc = _build_cached()
    res = run_bass_kernel_spmd(nc, in_maps, core_ids=list(range(NCORES)))
    out = np.concatenate([res.results[c]["outcol"] for c in range(NCORES)],
                         axis=1)
    if DEBUG:
        kernel._dbg = res.results
    return out.reshape(1, T, D)
